# version 04
# baseline (speedup 1.0000x reference)
import numpy as np
from contextlib import ExitStack

import concourse.bass as bass
import concourse.mybir as mybir
from concourse.alu_op_type import AluOpType as OP
from concourse.bass_utils import run_bass_kernel_spmd

F32 = mybir.dt.float32
BF16 = mybir.dt.bfloat16
I32 = mybir.dt.int32
AF = mybir.ActivationFunctionType
AX = mybir.AxisListType.X

DM, S = 4096, 4096
NH, KVH, HD, HALF = 32, 8, 128, 64
TB, NHC, ICN, NE = 32, 32, 56, 16
OWN = 1024
WINA, WINB = 16, 32
NPAIR = 16
INVD = 0.08838834764831845
EPS = 1e-5
LN_INVD = float(np.log(INVD))
THETA = 10000.0

_DEBUG = []
_PHASES = 5


class Sem:
    def __init__(self, h):
        self.h = h
        self.n = 0


class Build:
    ENGINES = ("sync", "gpsimd", "tensor", "vector", "scalar")

    def __init__(self):
        self.q = {e: [] for e in self.ENGINES}

    def wait(self, eng, sem, n=None):
        v = sem.n if n is None else n
        if v > 0:
            self.q[eng].append(lambda E, s=sem.h, vv=v: E.wait_ge(s, vv))

    def do(self, eng, fn):
        self.q[eng].append(fn)

    def inc(self, eng, sem, k, fn):
        sem.n += k
        self.q[eng].append(lambda E, f=fn, s=sem.h, kk=k: f(E).then_inc(s, kk))


def build_nc():
    nc = bass.Bass("TRN2", target_bir_lowering=False)

    dx = nc.dram_tensor("x_full", [S, DM], F32, kind="ExternalInput").ap()
    dxq = nc.dram_tensor("xq_nat", [OWN, DM], F32, kind="ExternalInput").ap()
    dxqT = nc.dram_tensor("xq_T", [DM, OWN], F32, kind="ExternalInput").ap()
    dwkv_c = nc.dram_tensor("wkv_codes", [DM, 2048], I32, kind="ExternalInput").ap()
    dwkv_s = nc.dram_tensor("wkv_scales", [32, 2048], F32, kind="ExternalInput").ap()
    dwq_c = nc.dram_tensor("wq_tiled", [32, 32, 128, 128], I32, kind="ExternalInput").ap()
    dwq_s = nc.dram_tensor("wq_srows", [32, 4096], F32, kind="ExternalInput").ap()
    dwo_c = nc.dram_tensor("wo_tiled", [32, 32, 128, 128], I32, kind="ExternalInput").ap()
    dwo_s = nc.dram_tensor("wo_srows", [32, 4096], F32, kind="ExternalInput").ap()
    du = nc.dram_tensor("u_tiled", [ICN, 32, 128, 128], F32, kind="ExternalInput").ap()
    dg = nc.dram_tensor("g_tiled", [ICN, 32, 128, 128], F32, kind="ExternalInput").ap()
    dd = nc.dram_tensor("d_tiled", [32, ICN, 128, 128], F32, kind="ExternalInput").ap()
    dgw = nc.dram_tensor("gw_t", [DM, NE], F32, kind="ExternalInput").ap()
    dcos = nc.dram_tensor("cos64", [S, HALF], F32, kind="ExternalInput").ap()
    dsin = nc.dram_tensor("sin64", [S, HALF], F32, kind="ExternalInput").ap()
    dcosqF = nc.dram_tensor("cosqF", [128, OWN], F32, kind="ExternalInput").ap()
    dsinqF = nc.dram_tensor("sinqF", [128, OWN], F32, kind="ExternalInput").ap()
    dswap = nc.dram_tensor("swapT", [128, 128], F32, kind="ExternalInput").ap()
    dmask = nc.dram_tensor("masks", [32, 128, 512], BF16, kind="ExternalInput").ap()
    dsmat = nc.dram_tensor("smat", [NE, ICN * 128], F32, kind="ExternalInput").ap()
    diden = nc.dram_tensor("iden", [128, 128], F32, kind="ExternalInput").ap()

    dout = nc.dram_tensor("out", [DM, OWN], F32, kind="ExternalOutput").ap()

    kt_h = nc.dram_tensor("kt_scratch", [KVH, TB, 128, 128], BF16).ap()
    v_h = nc.dram_tensor("v_scratch", [KVH, TB, 128, 128], BF16).ap()
    res_h = nc.dram_tensor("res_scratch", [32, 128, OWN], F32).ap()

    dbg = {}
    for name, shape, dt in [
        ("dbg_kt", [KVH * TB * 128, 128], BF16),
        ("dbg_v", [KVH * TB * 128, 128], BF16),
        ("dbg_qT", [128, NH * OWN], BF16),
        ("dbg_OT", [128, NH * OWN], BF16),
        ("dbg_res", [32 * 128, OWN], F32),
        ("dbg_nb", [128, NPAIR], F32),
        ("dbg_wT", [16, 8 * 128], BF16),
        ("dbg_hT0", [128, ICN * 512], BF16),
        ("dbg_rows", [1, 1280], F32),
        ("dbg_fa", [128, 96], F32),
        ("dbg_w16", [128, 16], BF16),
    ]:
        if name in _DEBUG:
            dbg[name] = nc.dram_tensor(name, shape, dt, kind="ExternalOutput").ap()

    B = Build()
    sy, g, te, ve, sc = "sync", "gpsimd", "tensor", "vector", "scalar"

    es = ExitStack()
    with es:
        BFA = es.enter_context(nc.sbuf_tensor("BFA", [128, 94208], BF16))
        FA = es.enter_context(nc.sbuf_tensor("FA", [128, 96], F32))
        s_iden = es.enter_context(nc.sbuf_tensor("s_iden", [128, 128], BF16))
        s_swap = es.enter_context(nc.sbuf_tensor("s_swap", [128, 128], BF16))
        s_ones_c = es.enter_context(nc.sbuf_tensor("s_ones_c", [1, 128], BF16))
        s_ones_p = es.enter_context(nc.sbuf_tensor("s_ones_p", [128, 1], BF16))
        s_half = es.enter_context(nc.sbuf_tensor("s_half", [1, 128], BF16))
        s_gw = es.enter_context(nc.sbuf_tensor("s_gw", [128, NHC * NE], BF16))
        s_wT = es.enter_context(nc.sbuf_tensor("s_wT", [16, 8 * 128], BF16))
        s_norm = es.enter_context(nc.sbuf_tensor("s_norm", [128, 264], F32))
        s_rows = es.enter_context(nc.sbuf_tensor("s_rows", [1, 1280], F32))
        s_rowb = es.enter_context(nc.sbuf_tensor("s_rowb", [1, 1024], BF16))
        s_nb = es.enter_context(nc.sbuf_tensor("s_nb", [128, NPAIR], F32))
        s_srow = es.enter_context(nc.sbuf_tensor("s_srow", [1, 4096], BF16))
        s_g16 = es.enter_context(nc.sbuf_tensor("s_g16", [128, 32], F32))
        s_junk = es.enter_context(nc.sbuf_tensor("s_junk", [128, 1536], BF16))

        psA = es.enter_context(nc.psum_tensor("psA", [128, 2048], F32))
        psB = es.enter_context(nc.psum_tensor("psB", [128, 2048], F32))
        psA_ap = psA.ap()
        psB_ap = psB.ap()
        psB_bf = psB.ap().bitcast(BF16)

        sds = Sem(es.enter_context(nc.semaphore("sds")))
        sdg = Sem(es.enter_context(nc.semaphore("sdg")))
        spe = Sem(es.enter_context(nc.semaphore("spe")))
        sdv = Sem(es.enter_context(nc.semaphore("sdv")))
        sac = Sem(es.enter_context(nc.semaphore("sac")))
        sgc = Sem(es.enter_context(nc.semaphore("sgc")))

        bfap = BFA.ap()
        bffap = BFA.ap().bitcast(F32)
        faap = FA.ap()
        bf = lambda o, n: bfap[:, o:o + n]
        bfp = lambda p0, p1, o, n: bfap[p0:p1, o:o + n]
        bff = lambda o, n: bffap[:, o:o + n]
        fa = lambda o, n: faap[:, o:o + n]
        fap = lambda p0, p1, o, n: faap[p0:p1, o:o + n]

        def jd(E):
            return E.tensor_copy(s_junk.ap()[:, 0:768], s_junk.ap()[:, 768:1536])

        def ja(E):
            return E.activation(s_junk.ap()[:, 0:768], s_junk.ap()[:, 768:1536],
                                AF.Copy)

        def dsmall(fn, inc=0):
            B.do(ve, fn)
            if inc:
                B.inc(ve, sdv, inc, jd)
            else:
                B.do(ve, jd)

        def asmall(fn, inc=0):
            B.do(sc, fn)
            if inc:
                B.inc(sc, sac, inc, ja)
            else:
                B.do(sc, ja)

        O_WKV, O_CB, O_XB, O_XNT = 0, 65536, 69632, 77824
        O_KROPE, O_COS8, O_SIN8, O_KTB, O_VTB = 86016, 87040, 87552, 88064, 89088
        O_COS64, O_SIN64 = 90112, 92160
        O_QT, O_XNQ, O_WQB, O_COSQ, O_SINQ, O_P2XB, O_RTMP, O_SQ2 = \
            0, 32768, 65536, 73728, 74752, 77824, 86016, 88064
        O_OT, O_MSK, O_KSB, O_VSB, O_PT, O_ACC, O_RB = \
            32768, 65536, 81920, 86016, 90112, 92160, 93184
        O_WOB, O_WOS = 0, 8192
        O_HT, O_XNQ5, O_UB, O_GB, O_SGT, O_TT5, O_WBC = \
            0, 28672, 45056, 53248, 61440, 62464, 63488
        O_DB = 45056
        F_LN, F_GT, F_GT2, F_E1, F_E2, F_GM = 0, 8, 24, 40, 56, 72
        F_XQT, F_RES = 4096, 6144
        F_XQT5, F_RES5 = 35840, 36864
        O_SMAT = 64512

        B.inc(g, sdg, 16, lambda E: E.dma_start(s_iden.ap(), diden))
        B.inc(g, sdg, 16, lambda E: E.dma_start(s_swap.ap(), dswap))
        B.inc(g, sdg, 16, lambda E: E.dma_start(
            s_gw.ap().rearrange("p (c n) -> p c n", n=NE),
            dgw.rearrange("(c p) n -> p c n", p=128)))
        B.inc(g, sdg, 16, lambda E: E.dma_start(
            bf(O_COS64, 2048).rearrange("p (t f) -> p t f", f=HALF),
            dcos.rearrange("(t p) f -> p t f", p=128)))
        B.inc(g, sdg, 16, lambda E: E.dma_start(
            bf(O_SIN64, 2048).rearrange("p (t f) -> p t f", f=HALF),
            dsin.rearrange("(t p) f -> p t f", p=128)))
        P0_GD = sdg.n
        B.inc(g, sgc, 1, lambda E: E.memset(s_ones_c.ap(), 1.0))
        B.inc(g, sgc, 1, lambda E: E.memset(s_ones_p.ap(), 1.0))
        B.inc(g, sgc, 1, lambda E: E.memset(s_half.ap(), 0.5))
        P0_GC = sgc.n

        deq_done = []
        bc_pe = []
        for hc in range(NHC):
            buf = hc % 2
            if hc >= 2:
                B.wait(g, sdv, deq_done[hc - 2])
                B.wait(g, spe, bc_pe[hc - 2])
            B.inc(g, sdg, 16, lambda E, hc=hc, buf=buf: E.dma_start(
                bf(O_CB + buf * 2048, 2048), dwkv_c[hc * 128:(hc + 1) * 128, :]))
            B.inc(g, sdg, 16, lambda E, hc=hc, buf=buf: E.dma_start(
                s_srow.ap()[0:1, buf * 2048:buf * 2048 + 2048],
                dwkv_s[hc:hc + 1, :]))
            ps = psA_ap if buf == 0 else psB_ap
            if hc == 0:
                B.wait(te, sgc, P0_GC)
            B.wait(te, sdg, sdg.n)
            if hc >= 2:
                B.wait(te, sdv, deq_done[hc - 2])
            for q4 in range(4):
                fn = lambda E, hc=hc, ps=ps, buf=buf, q4=q4: E.matmul(
                    ps[:, q4 * 512:(q4 + 1) * 512], s_half.ap(),
                    s_srow.ap()[0:1, buf * 2048 + q4 * 512:buf * 2048 + (q4 + 1) * 512],
                    start=True, stop=True)
                (B.inc(te, spe, 1, fn) if q4 == 3 else B.do(te, fn))
            bc_pe.append(spe.n)
            B.wait(ve, sdg, sdg.n)
            B.wait(ve, spe, spe.n)
            B.inc(ve, sdv, 1, lambda E, hc=hc, buf=buf, ps=ps: E.scalar_tensor_tensor(
                bf(O_WKV + hc * 2048, 2048), bf(O_CB + buf * 2048, 2048), 8.0,
                ps[:, :], OP.subtract, OP.mult))
            deq_done.append(sdv.n)
        P1A_DV = sdv.n

        def emit_ln(xoff, dummy_off):
            B.do(sc, lambda E, o=xoff, d=dummy_off: E.activation(
                bf(d, 4096), bf(o, 4096), AF.Copy, accum_out=fa(F_LN + 0, 1)))
            B.do(sc, lambda E, o=xoff, d=dummy_off: E.activation(
                bf(d, 4096), bf(o, 4096), AF.Square, accum_out=fa(F_LN + 1, 1)))
            B.inc(sc, sac, 1, ja)
            B.wait(ve, sac, sac.n)
            dsmall(lambda E: E.tensor_scalar(
                fa(F_LN + 2, 1), fa(F_LN + 0, 1), 1.0 / 4096, None, OP.mult))
            dsmall(lambda E: E.tensor_scalar(
                fa(F_LN + 3, 1), fa(F_LN + 1, 1), 1.0 / 4096, None, OP.mult))
            dsmall(lambda E: E.tensor_tensor(
                fa(F_LN + 4, 1), fa(F_LN + 2, 1), fa(F_LN + 2, 1), OP.mult))
            dsmall(lambda E: E.tensor_tensor(
                fa(F_LN + 5, 1), fa(F_LN + 3, 1), fa(F_LN + 4, 1), OP.subtract),
                inc=1)
            B.wait(sc, sdv, sdv.n)
            asmall(lambda E: E.activation(
                fa(F_LN + 6, 1), fa(F_LN + 5, 1), AF.Ln, bias=np.float32(EPS)))
            asmall(lambda E: E.activation(
                fa(F_LN + 7, 1), fa(F_LN + 6, 1), AF.Exp, scale=-0.5), inc=1)
            B.wait(ve, sac, sac.n)
            B.inc(ve, sdv, 1, lambda E, o=xoff: E.tensor_scalar(
                bf(o, 4096), bf(o, 4096), fa(F_LN + 2, 1), fa(F_LN + 7, 1),
                OP.subtract, OP.mult))

        def emit_xnt(xoff, dst_col_fn):
            evac_hist = []
            for grp in range(8):
                view = grp % 2
                if grp >= 2:
                    B.wait(te, sdv, evac_hist[grp - 2])
                for t4 in range(4):
                    hc = grp * 4 + t4
                    fn = lambda E, x=xoff, hc=hc, view=view, t4=t4: E.transpose(
                        psB_bf[:, view * 1024 + t4 * 128:view * 1024 + (t4 + 1) * 128],
                        bf(x + hc * 128, 128), s_iden.ap())
                    (B.inc(te, spe, 1, fn) if t4 == 3 else B.do(te, fn))
                B.wait(ve, spe, spe.n)
                for t4 in range(4):
                    hc = grp * 4 + t4
                    fn = lambda E, hc=hc, view=view, t4=t4, dc=dst_col_fn(hc): E.tensor_copy(
                        bf(dc, 128),
                        psB_bf[:, view * 1024 + t4 * 128:view * 1024 + (t4 + 1) * 128])
                    (B.inc(ve, sdv, 1, fn) if t4 == 3 else B.do(ve, fn))
                evac_hist.append(sdv.n)

        kvmm_done = [0] * TB
        psA_free = [0] * TB
        ktb_out = [0] * TB
        xb_ready = [0] * TB
        xnt_done = [0] * TB

        def p1_ln_xnt(tb):
            xo = O_XB + (tb % 2) * 4096
            xnto = O_XNT + (tb % 2) * 4096
            B.wait(sc, sdg, xb_ready[tb])
            if tb >= 2:
                B.wait(sc, spe, kvmm_done[tb - 2])
            emit_ln(xo, xnto)
            B.wait(te, sdv, sdv.n)
            emit_xnt(xo, lambda hc, xnto=xnto: xnto + hc * 128)
            xnt_done[tb] = spe.n

        def p1_xb_dma(tb):
            xo = O_XB + (tb % 2) * 4096
            if tb >= 2:
                B.wait(g, spe, xnt_done[tb - 2] or 0)
            B.inc(g, sdg, 16, lambda E, tb=tb, xo=xo: E.dma_start(
                bf(xo, 4096), dx[tb * 128:(tb + 1) * 128, :]))
            xb_ready[tb] = sdg.n

        p1_xb_dma(0)
        p1_xb_dma(1)
        p1_ln_xnt(0)
        for tb in range(TB):
            xnto = O_XNT + (tb % 2) * 4096
            if tb == 0:
                B.wait(te, sdv, P1A_DV)
            else:
                B.wait(te, sdv, psA_free[tb - 1])
            for hc in range(NHC):
                for q4 in range(4):
                    fn = lambda E, xnto=xnto, hc=hc, q4=q4: E.matmul(
                        psA_ap[:, q4 * 512:(q4 + 1) * 512],
                        bf(xnto + hc * 128, 128),
                        bf(O_WKV + hc * 2048 + q4 * 512, 512),
                        start=(hc == 0), stop=(hc == NHC - 1))
                    (B.inc(te, spe, 1, fn) if (hc == NHC - 1 and q4 == 3)
                     else B.do(te, fn))
            kvmm_done[tb] = spe.n
            if tb == 0:
                B.wait(ve, sdg, P0_GD)
            if tb >= 1:
                B.wait(ve, sds, ktb_out[tb - 1])
            for h8 in range(8):
                B.do(ve, lambda E, tb=tb, h8=h8: E.tensor_copy(
                    bf(O_COS8 + h8 * 64, 64), bf(O_COS64 + tb * 64, 64)))
                fn = lambda E, tb=tb, h8=h8: E.tensor_copy(
                    bf(O_SIN8 + h8 * 64, 64), bf(O_SIN64 + tb * 64, 64))
                (B.inc(ve, sdv, 1, fn) if h8 == 7 else B.do(ve, fn))
            B.wait(ve, sdv, sdv.n)
            B.wait(ve, spe, kvmm_done[tb])
            hsplit = lambda o, a2, b2: bf(o, 1024).rearrange(
                "p (h f) -> p h f", h=8)[:, :, a2:b2]
            px = lambda a2, b2: psA_ap[:, 0:1024].rearrange(
                "p (h f) -> p h f", h=8)[:, :, a2:b2]
            r8 = lambda o: bf(o, 512).rearrange("p (h f) -> p h f", h=8)
            B.do(ve, lambda E: E.tensor_tensor(r8(O_KTB), px(0, 64), r8(O_COS8), OP.mult))
            B.do(ve, lambda E: E.tensor_tensor(r8(O_VTB), px(64, 128), r8(O_SIN8), OP.mult))
            B.do(ve, lambda E: E.tensor_tensor(
                hsplit(O_KROPE, 0, 64), r8(O_KTB), r8(O_VTB), OP.subtract))
            B.do(ve, lambda E: E.tensor_tensor(r8(O_KTB), px(64, 128), r8(O_COS8), OP.mult))
            B.do(ve, lambda E: E.tensor_tensor(r8(O_VTB), px(0, 64), r8(O_SIN8), OP.mult))
            B.do(ve, lambda E: E.tensor_tensor(
                hsplit(O_KROPE, 64, 128), r8(O_KTB), r8(O_VTB), OP.add))
            B.inc(ve, sdv, 1, lambda E: E.tensor_copy(
                bf(O_VTB, 1024), psA_ap[:, 1024:2048]))
            psA_free[tb] = sdv.n
            KROPE_DV = sdv.n
            if tb + 1 < TB:
                if tb + 2 < TB:
                    p1_xb_dma(tb + 2)
                p1_ln_xnt(tb + 1)
            B.wait(sc, sdv, KROPE_DV)
            B.inc(sc, sac, 1, lambda E: E.activation(
                bf(O_CB, 1024), bf(O_KROPE, 1024), AF.Square))
            B.wait(ve, sac, sac.n)
            B.inc(ve, sdv, 1, lambda E, tb=tb: E.tensor_reduce(
                s_norm.ap()[:, tb * 8:(tb + 1) * 8],
                bf(O_CB, 1024).rearrange("p (h f) -> p h f", h=8), AX, OP.add))
            kt_evac = []
            for h in range(8):
                base = 2048 + (h % 2) * 1024
                if h >= 2:
                    B.wait(te, sdv, kt_evac[h - 2])
                if h == 0:
                    B.wait(te, sdv, KROPE_DV)
                B.inc(te, spe, 1, lambda E, h=h, base=base: E.transpose(
                    psB_bf[:, base:base + 128], bf(O_KROPE + h * 128, 128),
                    s_iden.ap()))
                B.wait(ve, spe, spe.n)
                B.inc(ve, sdv, 1, lambda E, h=h, base=base: E.tensor_copy(
                    bf(O_KTB + h * 128, 128), psB_bf[:, base:base + 128]))
                kt_evac.append(sdv.n)
            B.wait(sy, sdv, sdv.n)
            B.inc(sy, sds, 16, lambda E, tb=tb: E.dma_start(
                kt_h[:, tb].rearrange("h p n -> p h n"),
                bf(O_KTB, 1024).rearrange("p (h n) -> p h n", h=8)))
            B.inc(sy, sds, 16, lambda E, tb=tb: E.dma_start(
                v_h[:, tb].rearrange("h p n -> p h n"),
                bf(O_VTB, 1024).rearrange("p (h n) -> p h n", h=8)))
            ktb_out[tb] = sds.n
        P1_DS = sds.n

        B.wait(ve, sdv, sdv.n)
        B.do(ve, jd)
        dsmall(lambda E: E.tensor_reduce(
            s_norm.ap()[:, 256:264],
            s_norm.ap()[:, 0:256].rearrange("p (t h) -> p h t", h=8), AX, OP.max))
        dsmall(lambda E: E.tensor_copy(bf(O_COS8, 8), s_norm.ap()[:, 256:264]),
               inc=1)
        B.wait(te, sdv, sdv.n)
        B.inc(te, spe, 1, lambda E: E.transpose(
            psB_bf[0:8, 2048:2176], bf(O_COS8, 8), s_iden.ap()))
        B.wait(ve, spe, spe.n)
        dsmall(lambda E: E.tensor_reduce(
            fap(0, 8, F_LN, 1), psB_bf[0:8, 2048:2176], AX, OP.max))
        dsmall(lambda E: E.tensor_copy(bfp(0, 8, O_COS8, 1),
                                       fap(0, 8, F_LN, 1)), inc=1)
        B.wait(te, sdv, sdv.n)
        B.inc(te, spe, 1, lambda E: E.transpose(
            psB_bf[0:1, 2048:2056], bfp(0, 8, O_COS8, 1), s_iden.ap()[0:8, 0:8]))
        B.wait(ve, spe, spe.n)
        dsmall(lambda E: E.tensor_copy(
            s_rows.ap()[0:1, 96:104], psB_bf[0:1, 2048:2056]), inc=1)

        BPE, BDV, BAC = spe.n, sdv.n, sac.n
        B.wait(te, sdv, BDV); B.wait(te, sac, BAC)
        B.wait(ve, spe, BPE); B.wait(ve, sac, BAC)
        B.wait(sc, spe, BPE); B.wait(sc, sdv, BDV)
        B.wait(g, spe, BPE); B.wait(g, sdv, BDV)
        if _PHASES >= 2:
            B.inc(g, sdg, 16, lambda E: E.dma_start(bf(O_COSQ, 1024), dcosqF))
            B.inc(g, sdg, 16, lambda E: E.dma_start(bf(O_SINQ, 1024), dsinqF))
            COSQ_GD = sdg.n

            for otb in range(8):
                xo = O_P2XB + (otb % 2) * 4096
                if otb >= 2:
                    B.wait(g, spe, spe.n)
                B.inc(g, sdg, 16, lambda E, otb=otb, xo=xo: E.dma_start(
                    bf(xo, 4096), dxq[otb * 128:(otb + 1) * 128, :]))
                B.wait(sc, sdg, sdg.n)
                emit_ln(xo, O_RTMP)
                B.wait(te, sdv, sdv.n)
                emit_xnt(xo, lambda hc, otb=otb: O_XNQ + hc * 1024 + otb * 128)
            XNQ_DV = sdv.n

            for otb in range(8):
                if otb == 0:
                    B.wait(te, sdv, XNQ_DV)
                else:
                    B.wait(te, sdv, sdv.n)
                for hc in range(NHC):
                    fn = lambda E, otb=otb, hc=hc: E.matmul(
                        psA_ap[:, 0:16], bf(O_XNQ + hc * 1024 + otb * 128, 128),
                        s_gw.ap()[:, hc * 16:(hc + 1) * 16],
                        start=(hc == 0), stop=(hc == NHC - 1))
                    (B.inc(te, spe, 1, fn) if hc == NHC - 1 else B.do(te, fn))
                B.wait(ve, spe, spe.n)
                dsmall(lambda E: E.tensor_copy(fa(F_GT, 16), psA_ap[:, 0:16]))
                dsmall(lambda E: E.tensor_reduce(fa(F_GM, 1), fa(F_GT, 16), AX,
                                                 OP.max))
                dsmall(lambda E: E.tensor_scalar(fa(F_E1, 16), fa(F_GT, 16),
                                                 fa(F_GM, 1), None, OP.is_ge))
                dsmall(lambda E: E.scalar_tensor_tensor(
                    fa(F_GT2, 16), fa(F_E1, 16), -1.0e9, fa(F_GT, 16),
                    OP.mult, OP.add))
                dsmall(lambda E: E.tensor_reduce(fa(F_GM + 1, 1), fa(F_GT2, 16),
                                                 AX, OP.max))
                dsmall(lambda E: E.tensor_scalar(fa(F_E2, 16), fa(F_GT2, 16),
                                                 fa(F_GM + 1, 1), None, OP.is_ge))
                dsmall(lambda E: E.tensor_tensor(
                    fa(F_GM + 2, 1), fa(F_GM, 1), fa(F_GM + 1, 1), OP.subtract),
                    inc=1)
                B.wait(sc, sdv, sdv.n)
                asmall(lambda E: E.activation(
                    fa(F_GM + 3, 1), fa(F_GM + 2, 1), AF.Tanh, scale=0.5), inc=1)
                B.wait(ve, sac, sac.n)
                dsmall(lambda E: E.tensor_scalar(fa(F_GM + 4, 1), fa(F_GM + 3, 1),
                                                 0.5, 0.5, OP.mult, OP.add))
                dsmall(lambda E: E.tensor_scalar(fa(F_GM + 5, 1), fa(F_GM + 4, 1),
                                                 -1.0, 1.0, OP.mult, OP.add))
                dsmall(lambda E: E.tensor_scalar(s_g16.ap()[:, 0:16], fa(F_E1, 16),
                                                 fa(F_GM + 4, 1), None, OP.mult))
                dsmall(lambda E: E.tensor_scalar(s_g16.ap()[:, 16:32], fa(F_E2, 16),
                                                 fa(F_GM + 5, 1), None, OP.mult))
                dsmall(lambda E: E.tensor_tensor(
                    bf(O_RTMP, 16), s_g16.ap()[:, 0:16], s_g16.ap()[:, 16:32],
                    OP.add), inc=1)
                B.wait(te, sdv, sdv.n)
                B.inc(te, spe, 1, lambda E: E.transpose(
                    psB_bf[0:16, 0:128], bf(O_RTMP, 16), s_iden.ap()))
                B.wait(ve, spe, spe.n)
                dsmall(lambda E, otb=otb: E.tensor_copy(
                    s_wT.ap()[0:16, otb * 128:(otb + 1) * 128],
                    psB_bf[0:16, 0:128]), inc=1)
            WT_DV = sdv.n

            qmm_done = [0] * 32
            qdone = [0] * 32
            bcq_pe = [0] * 32
            for qd in range(32):
                wb = O_WQB + (qd % 2) * 4096
                if qd >= 2:
                    B.wait(g, spe, qmm_done[qd - 2])
                B.inc(g, sdg, 16, lambda E, qd=qd, wb=wb: E.dma_start(
                    bf(wb, 4096).rearrange("p (c n) -> p c n", c=32),
                    dwq_c[qd].rearrange("c p n -> p c n")))
                if qd >= 1:
                    B.wait(g, spe, bcq_pe[qd - 1])
                B.inc(g, sdg, 16, lambda E, qd=qd: E.dma_start(
                    s_srow.ap()[0:1, 0:4096], dwq_s[qd:qd + 1, :]))
                WQ_GD = sdg.n
                for hh in range(2):
                    B.wait(te, sdg, WQ_GD)
                    B.wait(te, sdv, sdv.n)
                    for q4 in range(4):
                        fn = lambda E, qd=qd, hh=hh, q4=q4: E.matmul(
                            psB_ap[:, q4 * 512:(q4 + 1) * 512], s_half.ap(),
                            s_srow.ap()[0:1, hh * 2048 + q4 * 512:
                                        hh * 2048 + (q4 + 1) * 512],
                            start=True, stop=True)
                        (B.inc(te, spe, 1, fn) if q4 == 3 else B.do(te, fn))
                    if hh == 1:
                        bcq_pe[qd] = spe.n
                    B.wait(ve, sdg, WQ_GD)
                    B.wait(ve, spe, spe.n)
                    B.inc(ve, sdv, 1, lambda E, wb=wb, hh=hh: E.scalar_tensor_tensor(
                        bf(wb + hh * 2048, 2048), bf(wb + hh * 2048, 2048), 8.0,
                        psB_ap[:, :], OP.subtract, OP.mult))
                DEQ_DV = sdv.n
                qp = (qd % 2) * 1024
                B.wait(te, sdv, DEQ_DV)
                if qd >= 2:
                    B.wait(te, sdv, qdone[qd - 2])
                for hc in range(NHC):
                    for q2 in range(2):
                        fn = lambda E, wb=wb, hc=hc, qp=qp, q2=q2: E.matmul(
                            psA_ap[:, qp + q2 * 512:qp + (q2 + 1) * 512],
                            bf(wb + hc * 128, 128),
                            bf(O_XNQ + hc * 1024 + q2 * 512, 512),
                            start=(hc == 0), stop=(hc == NHC - 1))
                        (B.inc(te, spe, 1, fn) if (hc == NHC - 1 and q2 == 1)
                         else B.do(te, fn))
                qmm_done[qd] = spe.n
                B.wait(ve, spe, qmm_done[qd])
                B.inc(ve, sdv, 1, lambda E, qp=qp: E.tensor_copy(
                    bf(O_RTMP, 1024), psA_ap[:, qp:qp + 1024]))
                EV_DV = sdv.n
                B.wait(te, sdv, EV_DV)
                for q2 in range(2):
                    fn = lambda E, q2=q2: E.matmul(
                        psB_ap[:, 1024 + q2 * 512:1024 + (q2 + 1) * 512], s_swap.ap(),
                        bf(O_RTMP + q2 * 512, 512), start=True, stop=True)
                    (B.inc(te, spe, 1, fn) if q2 == 1 else B.do(te, fn))
                SW_PE = spe.n
                qto = O_QT + qd * 1024
                B.wait(ve, spe, SW_PE)
                if qd == 0:
                    B.wait(ve, sdg, COSQ_GD)
                B.do(ve, lambda E: E.tensor_tensor(
                    bf(O_RTMP + 1024, 1024), bf(O_RTMP, 1024), bf(O_COSQ, 1024), OP.mult))
                B.do(ve, lambda E: E.tensor_tensor(
                    bf(O_RTMP, 1024), psB_ap[:, 1024:2048], bf(O_SINQ, 1024), OP.mult))
                B.inc(ve, sdv, 1, lambda E, qto=qto: E.tensor_tensor(
                    bf(qto, 1024), bf(O_RTMP + 1024, 1024), bf(O_RTMP, 1024), OP.add))
                qdone[qd] = sdv.n
                B.wait(sc, sdv, qdone[qd])
                B.wait(sc, spe, spe.n)
                B.inc(sc, sac, 1, lambda E, qto=qto: E.activation(
                    bf(O_SQ2, 1024), bf(qto, 1024), AF.Square))
                B.wait(te, sac, sac.n)
                B.wait(te, sdv, qdone[qd])
                for q2 in range(2):
                    fn = lambda E, q2=q2: E.matmul(
                        psB_ap[0:1, 1024 + q2 * 512:1024 + (q2 + 1) * 512],
                        s_ones_p.ap(), bf(O_SQ2 + q2 * 512, 512),
                        start=True, stop=True)
                    (B.inc(te, spe, 1, fn) if q2 == 1 else B.do(te, fn))
                B.wait(ve, spe, spe.n)
                B.do(ve, lambda E, qd=qd: E.tensor_reduce(
                    s_rows.ap()[0:1, 128 + qd:129 + qd],
                    psB_ap[0:1, 1024:2048], AX, OP.max))
                B.inc(ve, sdv, 1, jd)

            B.wait(ve, sdv, sdv.n)
            dsmall(lambda E: E.tensor_tensor(
                s_rows.ap()[0:1, 1024:1040],
                s_rows.ap()[0:1, 128:160].rearrange("o (p t) -> o p t", t=2)[:, :, 0:1],
                s_rows.ap()[0:1, 128:160].rearrange("o (p t) -> o p t", t=2)[:, :, 1:2],
                OP.max))
            dsmall(lambda E: E.tensor_copy(
                s_rows.ap()[0:1, 1040:1056].rearrange("o (p t) -> o p t", t=2)[:, :, 0:1],
                s_rows.ap()[0:1, 96:104]))
            dsmall(lambda E: E.tensor_copy(
                s_rows.ap()[0:1, 1040:1056].rearrange("o (p t) -> o p t", t=2)[:, :, 1:2],
                s_rows.ap()[0:1, 96:104]))
            dsmall(lambda E: E.tensor_tensor(
                s_rows.ap()[0:1, 1056:1072], s_rows.ap()[0:1, 1024:1040],
                s_rows.ap()[0:1, 1040:1056], OP.mult), inc=1)
            B.wait(sc, sdv, sdv.n)
            asmall(lambda E: E.activation(s_rows.ap()[0:1, 1072:1088],
                                          s_rows.ap()[0:1, 1056:1072], AF.Ln))
            asmall(lambda E: E.activation(
                s_rows.ap()[0:1, 1088:1104], s_rows.ap()[0:1, 1072:1088],
                AF.Exp, scale=0.5, bias=np.float32(LN_INVD)), inc=1)
            B.wait(ve, sac, sac.n)
            dsmall(lambda E: E.tensor_scalar(s_rows.ap()[0:1, 1104:1120],
                                             s_rows.ap()[0:1, 1088:1104],
                                             -1.0, None, OP.mult))
            dsmall(lambda E: E.tensor_copy(s_rowb.ap()[0:1, 0:16],
                                           s_rows.ap()[0:1, 1104:1120]), inc=1)
            B.wait(te, sdv, sdv.n)
            B.inc(te, spe, 1, lambda E: E.matmul(
                psB_ap[:, 0:16], s_ones_c.ap(), s_rowb.ap()[0:1, 0:16],
                start=True, stop=True))
            B.wait(ve, spe, spe.n)
            dsmall(lambda E: E.tensor_copy(s_nb.ap(), psB_ap[:, 0:16]), inc=1)

        BPE, BDV, BAC = spe.n, sdv.n, sac.n
        B.wait(te, sdv, BDV); B.wait(te, sac, BAC)
        B.wait(ve, spe, BPE); B.wait(ve, sac, BAC)
        B.wait(sc, spe, BPE); B.wait(sc, sdv, BDV)
        B.wait(g, spe, BPE); B.wait(g, sdv, BDV)
        B.wait(sy, spe, BPE); B.wait(sy, sdv, BDV)
        if _PHASES >= 3:

            B.wait(sy, sds, P1_DS)
            B.inc(sy, sds, 16, lambda E: E.dma_start(
                bf(O_MSK, 16384).rearrange("p (c n) -> p c n", c=32),
                dmask.rearrange("c p n -> p c n")))
            MSK_DS = sds.n

            kv_load = 0
            last_avmm = 0
            last_pair_dv = 0
            for sb in range(2):
                win = WINA if sb == 0 else WINB
                moff = 16 * 512 if sb == 0 else 0
                for kvh in range(KVH):
                    B.wait(sy, spe, last_avmm)
                    B.inc(sy, sds, 16, lambda E, kvh=kvh, win=win: E.dma_start(
                        bf(O_KSB, win * 128).rearrange("p (c n) -> p c n", c=win),
                        kt_h[kvh, 0:win].rearrange("c p n -> p c n")))
                    B.inc(sy, sds, 16, lambda E, kvh=kvh, win=win: E.dma_start(
                        bf(O_VSB, win * 128).rearrange("p (c n) -> p c n", c=win),
                        v_h[kvh, 0:win].rearrange("c p n -> p c n")))
                    KV_DS = sds.n
                    for sub in range(2):
                        pair = kvh * 2 + sub
                        h0 = 4 * kvh + 2 * sub
                        B.wait(ve, spe, spe.n)
                        B.inc(ve, sdv, 1, lambda E: E.memset(bf(O_ACC, 1024), 0.0))
                        exp_hist = []
                        msk_hist = []
                        av_hist = []

                        def emit_av(c):
                            ptb = O_PT + (c % 2) * 1024
                            B.wait(te, sdv, msk_hist[c])
                            if c == 0:
                                B.wait(te, sdv, last_pair_dv)
                            for hl in range(2):
                                fn = lambda E, c=c, ptb=ptb, hl=hl, win=win: E.matmul(
                                    psB_ap[:, hl * 512:(hl + 1) * 512],
                                    bf(O_VSB + c * 128, 128),
                                    bf(ptb + hl * 512, 512),
                                    start=(c == 0), stop=(c == win - 1))
                                (B.inc(te, spe, 1, fn) if hl == 1 else B.do(te, fn))
                            av_hist.append(spe.n)

                        for c in range(win):
                            s2p = (c % 2) * 1024
                            ptb = O_PT + (c % 2) * 1024
                            if c == 0:
                                B.wait(te, sds, KV_DS)
                                B.wait(te, sds, MSK_DS)
                                B.wait(te, sdv, sdv.n)
                            if c >= 2:
                                B.wait(te, sac, exp_hist[c - 2])
                            for hl in range(2):
                                fn = lambda E, c=c, s2p=s2p, hl=hl, h0=h0, sb=sb: E.matmul(
                                    psA_ap[:, s2p + hl * 512:s2p + (hl + 1) * 512],
                                    bf(O_KSB + c * 128, 128),
                                    bf(O_QT + (h0 + hl) * 1024 + sb * 512, 512),
                                    start=True, stop=True)
                                (B.inc(te, spe, 1, fn) if hl == 1 else B.do(te, fn))
                            QK_PE = spe.n
                            if c >= 1:
                                emit_av(c - 1)
                            B.wait(sc, spe, QK_PE)
                            if c >= 2:
                                B.wait(sc, spe, av_hist[c - 2])
                                B.wait(sc, sdv, msk_hist[c - 2])
                            B.inc(sc, sac, 1, lambda E, s2p=s2p, ptb=ptb, pair=pair: E.activation(
                                bf(ptb, 1024), psA_ap[:, s2p:s2p + 1024], AF.Exp,
                                scale=INVD, bias=s_nb.ap()[:, pair:pair + 1]))
                            exp_hist.append(sac.n)
                            B.wait(ve, sac, sac.n)
                            mc = O_MSK + moff + c * 512
                            B.do(ve, lambda E, ptb=ptb, mc=mc: E.tensor_tensor(
                                bf(ptb, 512), bf(ptb, 512), bf(mc, 512), OP.mult))
                            B.do(ve, lambda E, ptb=ptb, mc=mc: E.tensor_tensor(
                                bf(ptb + 512, 512), bf(ptb + 512, 512), bf(mc, 512),
                                OP.mult))
                            B.inc(ve, sdv, 1, lambda E, ptb=ptb: E.tensor_tensor(
                                bf(O_ACC, 1024), bf(O_ACC, 1024), bf(ptb, 1024), OP.add))
                            msk_hist.append(sdv.n)
                        emit_av(win - 1)
                        last_avmm = spe.n
                        B.wait(te, sdv, sdv.n)
                        for q2 in range(2):
                            fn = lambda E, q2=q2: E.matmul(
                                psB_ap[0:1, 1024 + q2 * 512:1024 + (q2 + 1) * 512],
                                s_ones_p.ap(), bf(O_ACC + q2 * 512, 512),
                                start=True, stop=True)
                            (B.inc(te, spe, 1, fn) if q2 == 1 else B.do(te, fn))
                        B.wait(ve, spe, spe.n)
                        B.do(ve, lambda E: E.reciprocal(s_rows.ap()[0:1, 0:1024],
                                                        psB_ap[0:1, 1024:2048]))
                        B.do(ve, jd)
                        dsmall(lambda E: E.tensor_copy(
                            s_rowb.ap()[0:1, 0:1024], s_rows.ap()[0:1, 0:1024]),
                            inc=1)
                        for hl in range(2):
                            B.wait(te, sdv, sdv.n)
                            B.inc(te, spe, 1, lambda E, hl=hl: E.matmul(
                                psB_ap[:, 1536:2048], s_ones_c.ap(),
                                s_rowb.ap()[0:1, hl * 512:(hl + 1) * 512],
                                start=True, stop=True))
                            B.wait(ve, spe, spe.n)
                            B.inc(ve, sdv, 1, lambda E: E.tensor_copy(
                                bf(O_RB, 512), psB_ap[:, 1536:2048]))
                            B.inc(ve, sdv, 1, lambda E, hl=hl, h0=h0, sb=sb: E.tensor_tensor(
                                bf(O_OT + (h0 + hl) * 1024 + sb * 512, 512),
                                psB_ap[:, hl * 512:(hl + 1) * 512],
                                bf(O_RB, 512), OP.mult))
                        last_pair_dv = sdv.n

        BPE, BDV, BAC = spe.n, sdv.n, sac.n
        B.wait(te, sdv, BDV); B.wait(te, sac, BAC)
        B.wait(ve, spe, BPE); B.wait(ve, sac, BAC)
        B.wait(sc, spe, BPE); B.wait(sc, sdv, BDV)
        B.wait(g, spe, BPE); B.wait(g, sdv, BDV)
        B.wait(sy, spe, BPE); B.wait(sy, sdv, BDV)
        if _PHASES >= 4:

            womm_done = [0] * 32
            res_dv = [0] * 32
            bco_pe = [0] * 32
            for hb in range(32):
                wb = O_WOB + (hb % 2) * 4096
                if hb >= 2:
                    B.wait(g, spe, womm_done[hb - 2])
                B.inc(g, sdg, 16, lambda E, hb=hb, wb=wb: E.dma_start(
                    bf(wb, 4096).rearrange("p (c n) -> p c n", c=32),
                    dwo_c[hb].rearrange("c p n -> p c n")))
                if hb >= 1:
                    B.wait(g, spe, bco_pe[hb - 1])
                B.inc(g, sdg, 16, lambda E, hb=hb: E.dma_start(
                    s_srow.ap()[0:1, 0:4096], dwo_s[hb:hb + 1, :]))
                WO_GD = sdg.n
                if hb >= 2:
                    B.wait(sy, sdv, res_dv[hb - 2])
                B.inc(sy, sds, 16, lambda E, hb=hb: E.dma_start(
                    bff(F_XQT + (hb % 2) * 1024, 1024),
                    dxqT[hb * 128:(hb + 1) * 128, :]))
                XQ_DS = sds.n
                for hh in range(2):
                    B.wait(te, sdg, WO_GD)
                    B.wait(te, sdv, sdv.n)
                    for q4 in range(4):
                        fn = lambda E, hb=hb, hh=hh, q4=q4: E.matmul(
                            psB_ap[:, q4 * 512:(q4 + 1) * 512], s_half.ap(),
                            s_srow.ap()[0:1, hh * 2048 + q4 * 512:
                                        hh * 2048 + (q4 + 1) * 512],
                            start=True, stop=True)
                        (B.inc(te, spe, 1, fn) if q4 == 3 else B.do(te, fn))
                    if hh == 1:
                        bco_pe[hb] = spe.n
                    B.wait(ve, sdg, WO_GD)
                    B.wait(ve, spe, spe.n)
                    B.inc(ve, sdv, 1, lambda E, wb=wb, hh=hh: E.scalar_tensor_tensor(
                        bf(wb + hh * 2048, 2048), bf(wb + hh * 2048, 2048), 8.0,
                        psB_ap[:, :], OP.subtract, OP.mult))
                DEQ_DV = sdv.n
                ap_ = (hb % 2) * 1024
                B.wait(te, sdv, DEQ_DV)
                if hb >= 2:
                    B.wait(te, sdv, res_dv[hb - 2])
                for hc in range(NHC):
                    for q2 in range(2):
                        fn = lambda E, wb=wb, hc=hc, ap_=ap_, q2=q2: E.matmul(
                            psA_ap[:, ap_ + q2 * 512:ap_ + (q2 + 1) * 512],
                            bf(wb + hc * 128, 128),
                            bf(O_OT + hc * 1024 + q2 * 512, 512),
                            start=(hc == 0), stop=(hc == NHC - 1))
                        (B.inc(te, spe, 1, fn) if (hc == NHC - 1 and q2 == 1)
                         else B.do(te, fn))
                womm_done[hb] = spe.n
                B.wait(ve, spe, womm_done[hb])
                B.wait(ve, sds, XQ_DS)
                B.inc(ve, sdv, 1, lambda E, ap_=ap_, hb=hb: E.tensor_tensor(
                    bff(F_RES + (hb % 2) * 1024, 1024), psA_ap[:, ap_:ap_ + 1024],
                    bff(F_XQT + (hb % 2) * 1024, 1024), OP.add))
                res_dv[hb] = sdv.n
                B.wait(sy, sdv, res_dv[hb])
                B.inc(sy, sds, 16, lambda E, hb=hb: E.dma_start(
                    res_h[hb], bff(F_RES + (hb % 2) * 1024, 1024)))
            P4_DS = sds.n

        BPE, BDV, BAC = spe.n, sdv.n, sac.n
        B.wait(te, sdv, BDV); B.wait(te, sac, BAC)
        B.wait(ve, spe, BPE); B.wait(ve, sac, BAC)
        B.wait(sc, spe, BPE); B.wait(sc, sdv, BDV)
        B.wait(g, spe, BPE); B.wait(g, sdv, BDV)
        B.wait(sy, spe, BPE); B.wait(sy, sdv, BDV)
        if _PHASES >= 5:

            B.inc(g, sdg, 16, lambda E: E.dma_start(
                bfp(0, 16, O_SMAT, ICN * 128), dsmat))
            SMAT_GD = sdg.n
            for hf in range(2):
                for otb4 in range(4):
                    otb = hf * 4 + otb4
                    xo = O_UB + (otb4 % 2) * 4096
                    B.wait(g, spe, spe.n)
                    B.inc(g, sdg, 16, lambda E, otb=otb, xo=xo: E.dma_start(
                        bf(xo, 4096), dxq[otb * 128:(otb + 1) * 128, :]))
                    B.wait(sc, sdg, sdg.n)
                    emit_ln(xo, O_HT)
                    B.wait(te, sdv, sdv.n)
                    emit_xnt(xo, lambda hc, otb4=otb4: O_XNQ5 + hc * 512 + otb4 * 128)
                XNQ5_DV = sdv.n

                ic_done = [0] * ICN
                h_done = [0] * ICN
                for ic in range(ICN):
                    ub = O_UB + (ic % 2) * 4096
                    gb = O_GB + (ic % 2) * 4096
                    if ic >= 2:
                        B.wait(g, spe, ic_done[ic - 2])
                    else:
                        B.wait(g, spe, spe.n)
                    B.inc(g, sdg, 16, lambda E, ic=ic, ub=ub: E.dma_start(
                        bf(ub, 4096).rearrange("p (c n) -> p c n", c=32),
                        du[ic].rearrange("c p n -> p c n")))
                    B.inc(g, sdg, 16, lambda E, ic=ic, gb=gb: E.dma_start(
                        bf(gb, 4096).rearrange("p (c n) -> p c n", c=32),
                        dg[ic].rearrange("c p n -> p c n")))
                    UG_GD = sdg.n
                    pu = (ic % 2) * 1024
                    B.wait(te, sdg, UG_GD)
                    if ic == 0:
                        B.wait(te, sdv, XNQ5_DV)
                    if ic >= 2:
                        B.wait(te, sdv, h_done[ic - 2])
                    for hc in range(NHC):
                        B.do(te, lambda E, ub=ub, hc=hc, pu=pu: E.matmul(
                            psA_ap[:, pu:pu + 512], bf(ub + hc * 128, 128),
                            bf(O_XNQ5 + hc * 512, 512),
                            start=(hc == 0), stop=(hc == NHC - 1)))
                        fn = lambda E, gb=gb, hc=hc, pu=pu: E.matmul(
                            psA_ap[:, pu + 512:pu + 1024], bf(gb + hc * 128, 128),
                            bf(O_XNQ5 + hc * 512, 512),
                            start=(hc == 0), stop=(hc == NHC - 1))
                        (B.inc(te, spe, 1, fn) if hc == NHC - 1 else B.do(te, fn))
                    ic_done[ic] = spe.n
                    wp = (ic % 2) * 512
                    if hf == 0 and ic == 0:
                        B.wait(te, sdv, WT_DV)
                        B.wait(te, sdg, SMAT_GD)
                    B.wait(te, sdv, sdv.n if ic < 2 else h_done[ic - 2])
                    B.inc(te, spe, 1, lambda E, ic=ic, wp=wp, hf=hf: E.matmul(
                        psB_ap[:, wp:wp + 512],
                        bfp(0, 16, O_SMAT + ic * 128, 128),
                        s_wT.ap()[0:16, hf * 512:(hf + 1) * 512],
                        start=True, stop=True))
                    WBC_PE = spe.n
                    B.wait(sc, spe, ic_done[ic])
                    sgo = O_SGT + (ic % 2) * 512
                    B.inc(sc, sac, 1, lambda E, pu=pu, sgo=sgo: E.activation(
                        bf(sgo, 512), psA_ap[:, pu + 512:pu + 1024], AF.Silu))
                    B.wait(ve, spe, WBC_PE)
                    wbo = O_WBC + (ic % 2) * 512
                    B.do(ve, lambda E, wp=wp, wbo=wbo: E.tensor_copy(
                        bf(wbo, 512), psB_ap[:, wp:wp + 512]))
                    B.wait(ve, sac, sac.n)
                    B.do(ve, lambda E, pu=pu, sgo=sgo: E.tensor_tensor(
                        bf(O_TT5, 512), bf(sgo, 512), psA_ap[:, pu:pu + 512], OP.mult))
                    B.inc(ve, sdv, 1, lambda E, ic=ic, wbo=wbo: E.tensor_tensor(
                        bf(O_HT + ic * 512, 512), bf(O_TT5, 512), bf(wbo, 512), OP.mult))
                    h_done[ic] = sdv.n
                P5B_DV = sdv.n

                f_done = [0] * 32
                for hb in range(32):
                    db = O_DB + (hb % 2) * 7168
                    if hb >= 2:
                        B.wait(g, spe, f_done[hb - 2])
                    else:
                        B.wait(g, sdv, P5B_DV)
                    B.inc(g, sdg, 16, lambda E, hb=hb, db=db: E.dma_start(
                        bf(db, 7168).rearrange("p (c n) -> p c n", c=ICN),
                        dd[hb].rearrange("c p n -> p c n")))
                    DD_GD = sdg.n
                    if hb >= 2:
                        B.wait(sy, sdv, sdv.n)
                    B.inc(sy, sds, 16, lambda E, hb=hb, hf=hf: E.dma_start(
                        bff(F_XQT5 + (hb % 2) * 512, 512),
                        res_h[hb][:, hf * 512:(hf + 1) * 512]))
                    RS_DS = sds.n
                    fp = (hb % 2) * 512
                    B.wait(te, sdg, DD_GD)
                    if hb == 0:
                        B.wait(te, sdv, P5B_DV)
                    if hb >= 2:
                        B.wait(te, sdv, f_done[hb - 2])
                    for icc in range(ICN):
                        fn = lambda E, db=db, icc=icc, fp=fp: E.matmul(
                            psA_ap[:, fp:fp + 512], bf(db + icc * 128, 128),
                            bf(O_HT + icc * 512, 512),
                            start=(icc == 0), stop=(icc == ICN - 1))
                        (B.inc(te, spe, 1, fn) if icc == ICN - 1 else B.do(te, fn))
                    FM_PE = spe.n
                    B.wait(ve, spe, FM_PE)
                    B.wait(ve, sds, RS_DS)
                    B.inc(ve, sdv, 1, lambda E, fp=fp, hb=hb: E.tensor_tensor(
                        bff(F_RES5 + (hb % 2) * 512, 512), psA_ap[:, fp:fp + 512],
                        bff(F_XQT5 + (hb % 2) * 512, 512), OP.add))
                    f_done[hb] = sdv.n
                    B.wait(sy, sdv, f_done[hb])
                    B.inc(sy, sds, 16, lambda E, hb=hb, hf=hf: E.dma_start(
                        dout[hb * 128:(hb + 1) * 128, hf * 512:(hf + 1) * 512],
                        bff(F_RES5 + (hb % 2) * 512, 512)))
                BPE, BDV, BAC = spe.n, sdv.n, sac.n
                B.wait(te, sdv, BDV); B.wait(te, sac, BAC)
                B.wait(ve, spe, BPE); B.wait(ve, sac, BAC)
                B.wait(sc, spe, BPE); B.wait(sc, sdv, BDV)
                B.wait(g, spe, BPE); B.wait(g, sdv, BDV)
                B.wait(sy, spe, BPE); B.wait(sy, sdv, BDV)

        if "dbg_kt" in dbg:
            B.inc(sy, sds, 16, lambda E: E.dma_start(
                dbg["dbg_kt"], kt_h.rearrange("h t p n -> (h t p) n")))
        if "dbg_v" in dbg:
            B.inc(sy, sds, 16, lambda E: E.dma_start(
                dbg["dbg_v"], v_h.rearrange("h t p n -> (h t p) n")))
        if "dbg_qT" in dbg:
            B.inc(sy, sds, 16, lambda E: E.dma_start(dbg["dbg_qT"], bf(O_QT, NH * OWN)))
        if "dbg_OT" in dbg:
            B.inc(sy, sds, 16, lambda E: E.dma_start(dbg["dbg_OT"], bf(O_OT, NH * OWN)))
        if "dbg_res" in dbg:
            B.inc(sy, sds, 16, lambda E: E.dma_start(
                dbg["dbg_res"], res_h.rearrange("h p n -> (h p) n")))
        if "dbg_nb" in dbg:
            B.inc(sy, sds, 16, lambda E: E.dma_start(dbg["dbg_nb"], s_nb.ap()))
        if "dbg_wT" in dbg:
            B.inc(sy, sds, 16, lambda E: E.dma_start(dbg["dbg_wT"], s_wT.ap()))
        if "dbg_rows" in dbg:
            B.inc(sy, sds, 16, lambda E: E.dma_start(dbg["dbg_rows"], s_rows.ap()))
        if "dbg_fa" in dbg:
            B.inc(sy, sds, 16, lambda E: E.dma_start(dbg["dbg_fa"], faap[:, :]))
        if "dbg_w16" in dbg:
            B.inc(sy, sds, 16, lambda E: E.dma_start(dbg["dbg_w16"], bf(O_RTMP, 16)))

        B.wait(sy, sds, sds.n)

        block = es.enter_context(nc.Block())

        @block.sync
        def _(E):
            for f in B.q["sync"]:
                f(E)

        @block.gpsimd
        def _(E):
            for f in B.q["gpsimd"]:
                f(E)

        @block.tensor
        def _(E):
            for f in B.q["tensor"]:
                f(E)

        @block.vector
        def _(E):
            for f in B.q["vector"]:
                f(E)

        @block.scalar
        def _(E):
            for f in B.q["scalar"]:
                f(E)

    return nc


def _prep_shared(z):
    out = {}
    out["wkv_codes"] = np.ascontiguousarray(
        np.concatenate([z["L0_attn_k_codes"], z["L0_attn_v_codes"]], axis=1))
    out["wkv_scales"] = np.ascontiguousarray(
        np.concatenate([z["L0_attn_k_scales"], z["L0_attn_v_scales"]], axis=1))
    qc = z["L0_attn_q_codes"]
    out["wq_tiled"] = np.ascontiguousarray(
        qc.reshape(32, 128, 32, 128).transpose(2, 0, 1, 3))
    out["wq_srows"] = np.ascontiguousarray(
        z["L0_attn_q_scales"].reshape(32, 32, 128).transpose(1, 0, 2).reshape(32, 4096))
    oc = z["L0_attn_o_codes"]
    out["wo_tiled"] = np.ascontiguousarray(
        oc.reshape(32, 128, 32, 128).transpose(2, 0, 1, 3))
    out["wo_srows"] = np.ascontiguousarray(
        z["L0_attn_o_scales"].reshape(32, 32, 128).transpose(1, 0, 2).reshape(32, 4096))
    out["u_tiled"] = np.ascontiguousarray(
        z["L0_ffn_u"].reshape(ICN, 128, 32, 128).transpose(0, 2, 3, 1))
    out["g_tiled"] = np.ascontiguousarray(
        z["L0_ffn_g"].reshape(ICN, 128, 32, 128).transpose(0, 2, 3, 1))
    out["d_tiled"] = np.ascontiguousarray(
        z["L0_ffn_d"].reshape(32, 128, ICN, 128).transpose(0, 2, 3, 1))
    out["gw_t"] = np.ascontiguousarray(z["L0_ffn_gate_w"].T)
    inv = (1.0 / THETA ** (np.arange(HALF, dtype=np.float64) / HALF))
    ang = np.arange(S, dtype=np.float64)[:, None] * inv[None, :]
    out["cos64"] = np.cos(ang).astype(np.float32)
    out["sin64"] = np.sin(ang).astype(np.float32)
    sm = np.zeros((NE, ICN * 128), np.float32)
    for e in range(NE):
        sm[e, e * 448:(e + 1) * 448] = 1.0
    out["smat"] = sm
    out["iden"] = np.eye(128, dtype=np.float32)
    sw = np.zeros((128, 128), np.float32)
    sw[(np.arange(128) + 64) % 128, np.arange(128)] = 1.0
    out["swapT"] = sw
    return out


def _prep_core(x, shared, b, j):
    import ml_dtypes
    rows_a = slice(j * 512, (j + 1) * 512)
    rows_b = slice((j + 4) * 512, (j + 5) * 512)
    xb = x[b]
    core = dict(shared)
    core["x_full"] = np.ascontiguousarray(xb)
    xq = np.ascontiguousarray(np.concatenate([xb[rows_a], xb[rows_b]], axis=0))
    core["xq_nat"] = xq
    core["xq_T"] = np.ascontiguousarray(xq.T)
    pos = np.concatenate([np.arange(j * 512, (j + 1) * 512),
                          np.arange((j + 4) * 512, (j + 5) * 512)]).astype(np.float64)
    inv = (1.0 / THETA ** (np.arange(HALF, dtype=np.float64) / HALF))
    ang = inv[:, None] * pos[None, :]
    cosq = np.cos(ang)
    sinq = np.sin(ang)
    core["cosqF"] = np.concatenate([cosq, cosq], axis=0).astype(np.float32)
    core["sinqF"] = np.concatenate([-sinq, sinq], axis=0).astype(np.float32)
    PB = (j + 4) * 512
    c_idx = np.arange(32)[:, None, None]
    k_idx = np.arange(128)[None, :, None]
    q_idx = np.arange(512)[None, None, :]
    mask = (PB + q_idx >= c_idx * 128 + k_idx).astype(np.float32)
    core["masks"] = mask.astype(ml_dtypes.bfloat16)
    return core


_NC_CACHE = None


def kernel(**inputs):
    global _NC_CACHE
    x = np.asarray(inputs["L0_attn_x"])
    shared = _prep_shared(inputs)
    in_maps = []
    for c in range(8):
        b, j = c // 4, c % 4
        in_maps.append(_prep_core(x, shared, b, j))
    if _NC_CACHE is None:
        _NC_CACHE = build_nc()
    nc = _NC_CACHE
    res = run_bass_kernel_spmd(nc, in_maps, core_ids=list(range(8)))
    out = np.empty((2, S, DM), np.float32)
    for c in range(8):
        b, j = c // 4, c % 4
        o = res.results[c]["out"]
        out[b, j * 512:(j + 1) * 512] = o[:, 0:512].T
        out[b, (j + 4) * 512:(j + 5) * 512] = o[:, 512:1024].T
    return out



